# version 01 (final)
# baseline (speedup 1.0000x reference)
import dataclasses
import math
from contextlib import ExitStack

import numpy as np
import ml_dtypes

import concourse.bass as bass
import concourse.mybir as mybir
from concourse.bass_utils import run_bass_kernel_spmd

F32 = mybir.dt.float32
BF16 = mybir.dt.bfloat16
I32 = mybir.dt.int32
AF = mybir.ActivationFunctionType
ALU = mybir.AluOpType
AX = mybir.AxisListType

BF = ml_dtypes.bfloat16

B, T, D = 4, 4096, 1024
TOK = 2048
NT = TOK // 128
NKC = D // 128
L1_EXT = [17 + i for i in range(NT)]
L1_NPAIRS = sum(L1_EXT)

_DEV_TAPS = ()
B_PAIRS = 4
B_CC = True
_PHASES = ("A", "A2", "B", "C", "D", "D2", "F", "E", "G")



def _prep_core_inputs(core, I):
    b, m = core // 2, core % 2
    h0, t0 = m * 8, m * TOK
    x = np.asarray(I["L0_attn_x"])
    d = {}
    d["xf"] = np.ascontiguousarray(x[b])
    d["xo"] = np.ascontiguousarray(x[b, t0:t0 + TOK])
    wk = np.asarray(I["L0_attn_W_K"])[h0:h0 + 8]
    wq = np.asarray(I["L0_attn_W_Q"])[h0:h0 + 8]
    wv = np.asarray(I["L0_attn_W_V"])[h0:h0 + 8]
    d["wkqv"] = np.ascontiguousarray(np.stack(
        [w.transpose(2, 0, 1).reshape(D, 512) for w in (wk, wq, wv)]))
    d["wo"] = np.ascontiguousarray(np.asarray(I["L0_attn_W_O"]).T)
    d["ws1"] = np.ascontiguousarray(I["L0_ffn_Ws1"])
    d["ws2"] = np.ascontiguousarray(I["L0_ffn_Ws2"])
    d["w1"] = np.ascontiguousarray(I["L0_ffn_W1"])
    d["w2"] = np.ascontiguousarray(I["L0_ffn_W2"])
    d["wr"] = np.ascontiguousarray(I["L0_ffn_Wr"])
    l1w = np.stack(
        [np.asarray(I[f"L1_attn_w_{n}"]).transpose(2, 0, 1) for n in ("q", "k", "v")])
    l1wd = np.empty((128, 3, 8, 64), np.float32)
    for w in range(3):
        for hp in range(8):
            l1wd[0:64, w, hp, :] = l1w[w, 2 * hp]
            l1wd[64:128, w, hp, :] = l1w[w, 2 * hp + 1]
    d["l1wd"] = np.ascontiguousarray(l1wd)
    swp = np.zeros((128, 128), dtype=BF)
    for i in range(128):
        swp[i, i ^ 64] = 1
    d["swp"] = swp
    LG = np.log(10000.0)
    dd = np.arange(64, dtype=np.float64)
    ropek = np.empty((8, 2, 128, T), dtype=BF)
    ropeq = np.empty((8, 2, 128, TOK), dtype=BF)
    posk = np.arange(T, dtype=np.float64)
    posq = np.arange(t0, t0 + TOK, dtype=np.float64)
    for hp in range(8):
        fr = np.exp(-(8.0 * dd + hp) * LG / 512.0)
        for tbl, pos in ((ropek, posk), (ropeq, posq)):
            ang = np.float32(np.outer(fr, pos).astype(np.float32))
            c = np.cos(ang).astype(np.float32)
            sn = np.sin(ang).astype(np.float32)
            tbl[hp, 0, 0:64] = c
            tbl[hp, 0, 64:128] = c
            tbl[hp, 1, 0:64] = -sn
            tbl[hp, 1, 64:128] = sn
    d["ropek"] = ropek
    d["ropeq"] = ropeq
    d["krw"] = np.ascontiguousarray(np.stack(
        [np.asarray(I["L1_ffn_keys"]).T, np.asarray(I["L1_ffn_rw"]).T]))
    d["rbv"] = np.ascontiguousarray(np.asarray(I["L1_ffn_rb"]).reshape(1, 16))
    d["w1t"] = np.ascontiguousarray(np.asarray(I["L1_ffn_w1"]).transpose(0, 2, 1))
    d["w2t"] = np.ascontiguousarray(np.asarray(I["L1_ffn_w2"]).transpose(0, 2, 1))
    d["b1s"] = np.ascontiguousarray(
        np.asarray(I["L1_ffn_b1"]).reshape(16, 32, 128).transpose(2, 0, 1))
    d["b2s"] = np.ascontiguousarray(I["L1_ffn_b2"])
    d["eyeb"] = np.eye(128, dtype=BF)
    d["eyer"] = np.eye(128, dtype=np.float32)
    d["trilf"] = np.tril(np.ones((128, 128), dtype=np.float32))
    mab = np.zeros((128, L1_NPAIRS, 2), dtype=np.float32)
    col = 0
    for i in range(NT):
        g = m * NT + i
        for j in range(L1_EXT[i]):
            if j < g:
                mab[:, col, 1] = 1.0
            elif j == g:
                mab[:, col, 0] = 1.0
            col += 1
    d["mab"] = mab
    d["posf"] = np.arange(T, dtype=np.float32).reshape(1, T)
    d["poso"] = np.ascontiguousarray(
        np.arange(t0, t0 + TOK, dtype=np.float32).reshape(1, TOK))
    return d


_INPUT_SPECS = {
    "xf": ([T, D], F32), "xo": ([TOK, D], F32),
    "wkqv": ([3, D, 512], F32), "wo": ([D, D], F32),
    "ws1": ([D, 512], F32), "ws2": ([512, D], F32),
    "w1": ([8, D, 512], F32), "w2": ([8, 512, D], F32), "wr": ([D, 8], F32),
    "l1wd": ([128, 3, 8, 64], F32), "swp": ([128, 128], BF16),
    "ropek": ([8, 2, 128, T], BF16), "ropeq": ([8, 2, 128, TOK], BF16),
    "krw": ([2, D, 16], F32), "rbv": ([1, 16], F32),
    "w1t": ([16, D, 4096], F32), "w2t": ([16, 4096, D], F32),
    "b1s": ([128, 16, 32], F32), "b2s": ([16, D], F32),
    "eyeb": ([128, 128], BF16), "eyer": ([128, 128], F32),
    "trilf": ([128, 128], F32),
    "mab": ([128, L1_NPAIRS, 2], F32),
    "posf": ([1, T], F32), "poso": ([1, TOK], F32),
}


class Sems:
    def __init__(self, nc, ctx, names):
        self.h, self.c = {}, {}
        for n in names:
            self.h[n] = ctx.enter_context(nc.semaphore(n))
            self.c[n] = 0

    def inc(self, instr, name, n=1):
        instr.then_inc(self.h[name], n)
        self.c[name] += n
        return self.c[name]

    def wait(self, eng, name, v=None):
        eng.wait_ge(self.h[name], self.c[name] if v is None else v)

    def val(self, name):
        return self.c[name]


def ts(i, n=128):
    return slice(i * n, (i + 1) * n)


class Prog:
    def __init__(self):
        self.taps = {}


def build_program(phases=_PHASES, taps=()):
    P = Prog()
    nc = bass.Bass("TRN2", target_bir_lowering=False, debug=False)
    P.nc = nc
    ctx = ExitStack()
    P.ctx = ctx

    P.IN = {k: nc.dram_tensor(k, sh, dt, kind="ExternalInput").ap()
            for k, (sh, dt) in _INPUT_SPECS.items()}
    P.OUT = nc.dram_tensor("out", [TOK, D], F32, kind="ExternalOutput").ap()

    P.cin_z = nc.dram_tensor("cin_z", [512, T], BF16, kind="Internal")
    P.cout_z = nc.dram_tensor("cout_z", [2, 512, T], BF16, kind="Internal")
    P.cin_xt = nc.dram_tensor("cin_xt", [D, TOK], BF16, kind="Internal")
    P.cout_xt = nc.dram_tensor("cout_xt", [2, D, TOK], BF16, kind="Internal")
    P.vt_hbm = nc.dram_tensor("vt_hbm", [T, D], BF16, kind="Internal")

    P.devout = {nm: nc.dram_tensor("devtap_" + nm, sh, dt, kind="ExternalOutput").ap()
                for nm, sh, dt in taps}

    P.S = Sems(nc, ctx, ["ldA", "ldW", "tE", "vE", "sE", "cc", "st", "gp", "tp"])

    P.eyeb = nc.alloc_sbuf_tensor("sb_eyeb", [128, 128], BF16)
    P.trilf = nc.alloc_sbuf_tensor("sb_trilf", [128, 128], F32)
    P.stat = nc.alloc_sbuf_tensor("sb_stat", [128, 64], F32)
    P.omg = nc.alloc_sbuf_tensor("sb_omg", [128, NT, 16], F32)
    P.xsq = nc.alloc_sbuf_tensor("sb_xsq", [128, NT], F32)
    P.trash = nc.alloc_sbuf_tensor("sb_trash", [128, 2, 1024], BF16)
    P.krwf = nc.alloc_sbuf_tensor("sb_krwf", [128, 8, 32], F32)
    P.cmbr = nc.alloc_sbuf_tensor("sb_cmbr", [128, 8, 16], F32)
    P.ksqf = nc.alloc_sbuf_tensor("sb_ksqf", [128, 8, 16], F32)
    P.mst = nc.alloc_sbuf_tensor("sb_mst", [128, NT, 2], F32)
    P.eyer = nc.alloc_sbuf_tensor("sb_eyer", [128, 128], F32)
    P.b1s = nc.alloc_sbuf_tensor("sb_b1s", [128, 16, 32], F32)
    P.rbk = nc.alloc_sbuf_tensor("sb_rbk", [1, 16], F32)
    P.rbf = nc.alloc_sbuf_tensor("sb_rbf", [1, 16], F32)
    P.onec = nc.alloc_sbuf_tensor("sb_onec", [128, 1], F32)
    P.oner = nc.alloc_sbuf_tensor("sb_oner", [1, 128], F32)
    P.wtb = nc.alloc_sbuf_tensor("sb_wtb", [128, 16], BF16)
    P.wtt = nc.alloc_sbuf_tensor("sb_wtt", [16, 128], BF16)
    P.lg16 = nc.alloc_sbuf_tensor("sb_lg16", [128, 16], F32)
    P.lwb = nc.alloc_sbuf_tensor("sb_lwb", [128, 3, 8, 64], BF16)
    P.swpb = nc.alloc_sbuf_tensor("sb_swpb", [128, 128], BF16)
    P.cur16 = nc.alloc_sbuf_tensor("sb_cur16", [128, 16], F32)
    P.eq16 = nc.alloc_sbuf_tensor("sb_eq16", [128, 16], F32)

    ARENA = 196608
    P.arena = nc.alloc_sbuf_tensor("sb_arena", [128, ARENA], mybir.dt.uint8)

    def view(off, shape, dtype):
        nbytes = int(np.prod(shape[1:])) * mybir.dt.size(dtype)
        assert off % 32 == 0 and off + nbytes <= ARENA, (off, nbytes)
        v = P.arena.ap()[:, off:off + nbytes].bitcast(dtype)
        if len(shape) == 3:
            v = v.rearrange("p (a b) -> p a b", a=shape[1], b=shape[2])
        elif len(shape) == 4:
            v = v.rearrange("p (a b c) -> p a b c", a=shape[1], b=shape[2], c=shape[3])
        else:
            assert len(shape) == 2
        return v

    P.view = view

    P.PA = nc.alloc_psum_tensor("PA", [128, 1024], F32)
    P.PB = nc.alloc_psum_tensor("PB", [128, 1024], F32)
    P.PC = nc.alloc_psum_tensor("PC", [128, 1024], F32)
    P.PT = nc.alloc_psum_tensor("PT", [128, 1024], BF16)
    P.PS = nc.alloc_psum_tensor("PS", [128, 512], F32)

    P.x1res = view(0, [128, NT, D], F32)
    P.b2s_v = view(94208, [128, 1024], BF16)
    P.K_sb = view(0, [128, 4, T], BF16)
    P.Q_sb = view(32768, [128, 4, T], BF16)
    P.V_sb = view(65536, [128, 4, T], BF16)
    P.xnof = view(98304, [128, NKC, TOK], BF16)
    P.A_tr = 131072

    with nc.Block() as blk:
        P.blk = blk
        _phase_consts(P)
        if "A" in phases:
            _phase_A(P)
        if "A2" in phases:
            _phase_A2(P)
        if "B" in phases:
            _phase_B(P)
        if "C" in phases:
            _phase_C(P)
        if "D" in phases:
            _phase_D(P)
        if "D2" in phases:
            _phase_D2(P)
        if "F" in phases:
            _phase_F(P)
        if "E" in phases:
            _phase_E(P)
        _phase_G(P)
        _emit_taps(P)

    ctx.close()
    return nc


def _phase_consts(P):
    S, blk = P.S, P.blk

    @blk.vector
    def _(vector):
        vector.memset(P.stat.ap()[:, 62:63], -20.0)
        vector.memset(P.stat.ap()[:, 61:62], -4096.0 / 8.0 - 20.0)
        i = vector.memset(P.stat.ap()[:, 60:61], -32768.0 / 32.0)
        S.inc(i, "vE", 1)



def _emit_ln_transpose(P, src_hbm, nblocks, dst_writer,
                       tensor_extra=None, vector_extra=None, scalar_extra=None):
    nc, S, blk, view = P.nc, P.S, P.blk, P.view
    A = P.A_tr
    TPB = 2
    xblk = [view(A + i * 8192, [128, TPB, 1024], F32) for i in range(2)]
    xnt = [view(A + 16384 + i * 4096, [128, TPB, 1024], BF16) for i in range(2)]

    info = dict(
        b_ld=S.val("ldA"), b_v=S.val("vE"), b_s=S.val("sE"), b_t=S.val("tE"),
        V_TOT=2 + (vector_extra.per_block if vector_extra else 0),
        S_TOT=4 + (scalar_extra.per_block if scalar_extra else 0),
        T_TOT=TPB + (tensor_extra.per_block if tensor_extra else 0),
    )
    b_ld, b_v, b_s, b_t = info["b_ld"], info["b_v"], info["b_s"], info["b_t"]
    V_TOT, S_TOT, T_TOT = info["V_TOT"], info["S_TOT"], info["T_TOT"]

    @blk.sync
    def _(sync):
        src = src_hbm.rearrange("(bl tt p) d -> bl p tt d", bl=nblocks, p=128)
        for bi in range(nblocks):
            if bi >= 2:
                sync.wait_ge(S.h["vE"], b_v + (bi - 2) * V_TOT + 2)
            else:
                sync.wait_ge(S.h["vE"], b_v)
                sync.wait_ge(S.h["sE"], b_s)
            sync.wait_ge(S.h["ldA"], b_ld + 16 * bi)
            i = sync.dma_start(out=xblk[bi % 2][:, :, :], in_=src[bi])
            S.inc(i, "ldA", 16)

    @blk.scalar
    def _(scalar):
        for bi in range(nblocks):
            scalar.wait_ge(S.h["ldA"], b_ld + 16 * (bi + 1))
            last = None
            for j in range(TPB):
                last = scalar.activation(
                    out=P.trash.ap()[:, j % 2, :], in_=xblk[bi % 2][:, j, :],
                    func=AF.Square, accum_out=P.stat.ap()[:, 4 + j:5 + j])
            S.inc(last, "sE", 1)
            scalar.wait_ge(S.h["vE"], b_v + bi * V_TOT + 1)
            i = scalar.activation(out=P.stat.ap()[:, 16:16 + TPB],
                                  in_=P.stat.ap()[:, 12:12 + TPB], func=AF.Sqrt)
            S.inc(i, "sE", 1)
            for j in range(TPB):
                scalar.wait_ge(S.h["tE"], b_t + bi * T_TOT + j + 1)
                i = dst_writer(scalar, bi, j)
                S.inc(i, "sE", 1)
            if scalar_extra:
                scalar_extra(scalar, bi, info)

    @blk.vector
    def _(vector):
        inv = 1.0 / D
        for bi in range(nblocks):
            vector.wait_ge(S.h["ldA"], b_ld + 16 * (bi + 1))
            for j in range(TPB):
                vector.reduce_sum(P.stat.ap()[:, j:j + 1], xblk[bi % 2][:, j, :],
                                  axis=AX.X)
            vector.wait_ge(S.h["sE"], b_s + bi * S_TOT + 1)
            vector.drain()
            vector.tensor_scalar(P.stat.ap()[:, 8:8 + TPB], P.stat.ap()[:, 0:TPB],
                                 inv, None, ALU.mult)
            vector.tensor_scalar(P.stat.ap()[:, 12:12 + TPB], P.stat.ap()[:, 4:4 + TPB],
                                 inv, None, ALU.mult)
            vector.drain()
            vector.tensor_tensor(P.stat.ap()[:, 20:20 + TPB], P.stat.ap()[:, 8:8 + TPB],
                                 P.stat.ap()[:, 8:8 + TPB], ALU.mult)
            vector.drain()
            vector.tensor_sub(P.stat.ap()[:, 12:12 + TPB], P.stat.ap()[:, 12:12 + TPB],
                              P.stat.ap()[:, 20:20 + TPB])
            vector.drain()
            vector.tensor_scalar(P.stat.ap()[:, 12:12 + TPB], P.stat.ap()[:, 12:12 + TPB],
                                 1e-5, None, ALU.add)
            vector.drain()
            i = vector.reciprocal(P.stat.ap()[:, 12:12 + TPB], P.stat.ap()[:, 12:12 + TPB])
            S.inc(i, "vE", 1)
            vector.wait_ge(S.h["sE"], b_s + bi * S_TOT + 2)
            if bi >= 2:
                vector.wait_ge(S.h["tE"], b_t + (bi - 2) * T_TOT + TPB)
            else:
                vector.wait_ge(S.h["tE"], b_t)
            last = None
            for j in range(TPB):
                last = vector.tensor_scalar(
                    xnt[bi % 2][:, j, :], xblk[bi % 2][:, j, :],
                    P.stat.ap()[:, 8 + j:9 + j], P.stat.ap()[:, 16 + j:17 + j],
                    ALU.subtract, ALU.mult)
            S.inc(last, "vE", 1)
            if vector_extra:
                vector_extra(vector, bi, info)

    @blk.tensor
    def _(tensor):
        tensor.wait_ge(S.h["ldW"], getattr(P, "ldW_consts", 0))
        for bi in range(nblocks):
            tensor.wait_ge(S.h["vE"], b_v + bi * V_TOT + 2)
            for j in range(TPB):
                tensor.wait_ge(S.h["sE"], b_s + bi * S_TOT + 2 + j)
                last = None
                for k in range(NKC):
                    last = tensor.transpose(
                        P.PT.ap()[:, ts(k)], xnt[bi % 2][:, j, ts(k)],
                        P.eyeb.ap()[:, :])
                S.inc(last, "tE", 1)
            if tensor_extra:
                tensor_extra(tensor, bi, info)

    return info



def _phase_A(P):
    nc, S, blk, view = P.nc, P.S, P.blk, P.view
    A = P.A_tr
    xnb = [view(A + 24576 + i * 4096, [128, NKC, 256], BF16) for i in range(2)]
    wkqv = view(A + 32768, [128, 3, NKC, 512], BF16)

    b_ldW = S.val("ldW")

    @blk.gpsimd
    def _(g):
        i = g.dma_start(out=P.eyeb.ap()[:, :], in_=P.IN["eyeb"])
        S.inc(i, "ldW", 16)
        i = g.dma_start(out=P.trilf.ap()[:, :], in_=P.IN["trilf"])
        S.inc(i, "ldW", 16)
        w = P.IN["wkqv"].rearrange("w (k p) f -> w p k f", p=128)
        for wi in range(3):
            i = g.dma_start(out=wkqv[:, wi, :, :], in_=w[wi])
            S.inc(i, "ldW", 16)
        P.ldW_consts = S.val("ldW")

    def dst_writer(scalar, bi, j):
        return scalar.copy(
            xnb[bi % 2][:, :, ts(j)],
            P.PT.ap()[:, :].rearrange("p (a b) -> p a b", a=NKC, b=128))

    dsts = [P.K_sb, P.Q_sb, P.V_sb]
    regions = [(P.PA, 0), (P.PA, 512), (P.PB, 0), (P.PB, 512)]
    state = {"r": 0}
    vstate = {"r": 0}

    def tensor_extra(tensor, bi, info):
        if bi == 0:
            tensor.wait_ge(S.h["ldW"], P.ldW_consts)
        tensor.wait_ge(S.h["sE"], info["b_s"] + bi * info["S_TOT"] + 4)
        for wi in range(3):
            for mt in range(4):
                r = state["r"]
                pt, off = regions[r % 4]
                if r >= 4:
                    bq, wq_ = divmod(r - 4, 12)
                    tensor.wait_ge(
                        S.h["vE"],
                        info["b_v"] + bq * info["V_TOT"] + 2 + wq_ + 1)
                last = None
                for k in range(NKC):
                    last = tensor.matmul(
                        pt.ap()[:, off:off + 256],
                        wkqv[:, wi, k, ts(mt)],
                        xnb[bi % 2][:, k, :],
                        start=(k == 0), stop=(k == NKC - 1))
                S.inc(last, "tE", 1)
                state["r"] += 1

    tensor_extra.per_block = 12

    def vector_extra(vector, bi, info):
        for wi in range(3):
            for mt in range(4):
                r = vstate["r"]
                pt, off = regions[r % 4]
                bq, wq_ = divmod(r, 12)
                vector.wait_ge(
                    S.h["tE"], info["b_t"] + bq * info["T_TOT"] + 2 + wq_ + 1)
                i = vector.tensor_copy(
                    dsts[wi][:, mt, ts(bi, 256)], pt.ap()[:, off:off + 256])
                S.inc(i, "vE", 1)
                vstate["r"] += 1

    vector_extra.per_block = 12

    _emit_ln_transpose(P, P.IN["xf"], 16, dst_writer,
                       tensor_extra=tensor_extra, vector_extra=vector_extra)

    P.taps["K"] = P.K_sb[:, :, :]
    P.taps["Q"] = P.Q_sb[:, :, :]
    P.taps["V"] = P.V_sb[:, :, :]


def _phase_A2(P):
    def dst_writer(scalar, bi, j):
        return scalar.copy(
            P.xnof[:, :, ts(bi * 2 + j)],
            P.PT.ap()[:, :].rearrange("p (a b) -> p a b", a=NKC, b=128))

    _emit_ln_transpose(P, P.IN["xo"], 8, dst_writer)
    P.taps["xnof"] = P.xnof[:, :, :]


def _phase_B(P):
    nc, S, blk, view = P.nc, P.S, P.blk, P.view
    A = P.A_tr
    zacc = view(A, [128, T], F32)
    pstr = [view(A + 16384 + i * 8192, [128, T], BF16) for i in range(4)]
    vts = view(A + 49152, [128, 2, 128], BF16)
    zbf = view(A + 49664, [128, T], BF16)
    lpt = P.stat.ap()[:, 24:33]
    lpb = P.stat.ap()[:, 33:42]
    SH = 20.0
    DC = 4096.0

    regions = [(P.PA, 0), (P.PA, 512), (P.PB, 0), (P.PB, 512)]

    def chunks_of(ci):
        c0 = ci * 128
        out = []
        off = 0
        while off < c0:
            w = min(512, c0 - off)
            out.append((off, w, False))
            off += w
        out.append((c0, 128, True))
        return out

    b_t, b_v, b_s, b_g = S.val("tE"), S.val("vE"), S.val("sE"), S.val("gp")
    nch = {ci: len(chunks_of(ci)) for ci in range(32)}

    tE_at, sE_at, vE_at, tE_z, vE_z = {}, {}, {}, {}, {}
    t, v, sc = b_t, b_v, b_s
    for mt in range(B_PAIRS):
        for ci in range(32):
            t += nch[ci]
            tE_at[(mt, ci)] = t
            t += 1
            v += 1
            sc += 2 * nch[ci]
            sE_at[(mt, ci)] = sc
            v += 2
            vE_at[(mt, ci)] = v
            if ci % 2 == 1:
                g = ci // 2
                ncc = (ci * 128 + 128 + 511) // 512
                t += ncc
                tE_z[(mt, g)] = t
                v += ncc
                vE_z[(mt, g)] = v
        v += 1
    P.B_done = dict(tE=t, vE=v, sE=sc)

    def prev_tEz(mt, ci):
        if ci >= 2:
            return tE_z[(mt, (ci - 2) // 2)]
        if mt > 0:
            return tE_z[(mt - 1, 15)]
        return b_t

    @blk.tensor
    def _(tensor):
        tensor.wait_ge(S.h["vE"], b_v)
        tensor.wait_ge(S.h["sE"], b_s)
        for mt in range(B_PAIRS):
            for ci in range(32):
                ch = chunks_of(ci)
                sE_start = sE_at[(mt, ci)] - 2 * nch[ci]
                for q, (off, w, diag) in enumerate(ch):
                    if q >= 2:
                        tensor.wait_ge(S.h["sE"], sE_start + 2 * (q - 1))
                    elif (mt, ci) != (0, 0):
                        tensor.wait_ge(S.h["sE"], sE_start)
                    r0 = regions[(2 * q) % 4]
                    r1 = regions[(2 * q + 1) % 4]
                    ktile = P.K_sb[:, mt, ts(ci)]
                    tensor.matmul(r0[0].ap()[:, r0[1]:r0[1] + w],
                                  ktile[0:64, :],
                                  P.Q_sb[0:64, mt, off:off + w],
                                  start=True, stop=True)
                    i1 = tensor.matmul(r1[0].ap()[:, r1[1]:r1[1] + w],
                                       ktile[64:128, :],
                                       P.Q_sb[64:128, mt, off:off + w],
                                       start=True, stop=True)
                    S.inc(i1, "tE", 1)
                if (mt, ci) != (0, 0):
                    pv = vE_at[(mt, ci - 1)] if ci else vE_at[(mt - 1, 31)]
                    tensor.wait_ge(S.h["vE"], pv)
                i1 = tensor.transpose(P.PT.ap()[:, 0:128],
                                       P.V_sb[:, mt, ts(ci)],
                                       P.eyeb.ap()[:, :])
                S.inc(i1, "tE", 1)
                if ci % 2 == 1:
                    g = ci // 2
                    e0 = (ci - 1) * 128 + 128
                    e1 = ci * 128 + 128
                    ncc = (e1 + 511) // 512
                    tensor.wait_ge(S.h["vE"], vE_at[(mt, ci)])
                    for cc in range(ncc):
                        if cc >= 2:
                            tensor.wait_ge(S.h["vE"], vE_at[(mt, ci)] + cc - 1)
                        off = cc * 512
                        pco = 512 * (cc % 2)
                        w1 = min(512, e0 - off)
                        w2 = min(512, e1 - off)
                        widths = [(0, w1), (1, w2)]
                        live = sorted([(ct, wv) for ct, wv in widths if wv > 0],
                                      key=lambda x: -x[1])
                        for pos, (rows, stripbase) in enumerate(((slice(0, 64), 0),
                                                                (slice(64, 128), 2))):
                            for li, (ct, wv) in enumerate(live):
                                i1 = tensor.matmul(
                                    P.PC.ap()[rows, pco:pco + wv],
                                    vts[:, ct, rows],
                                    pstr[stripbase + ct][:, off:off + wv],
                                    start=(li == 0), stop=(li == len(live) - 1),
                                    tile_position=(0, 64 * pos))
                        S.inc(i1, "tE", 1)

    @blk.scalar
    def _(scalar):
        for mt in range(B_PAIRS):
            for ci in range(32):
                ch = chunks_of(ci)
                sl = ci % 2
                scalar.wait_ge(S.h["tE"], prev_tEz(mt, ci))
                if ci >= 1:
                    scalar.wait_ge(S.h["vE"], vE_at[(mt, ci - 1)] - 1)
                elif mt >= 1:
                    scalar.wait_ge(S.h["vE"], vE_at[(mt - 1, 31)] - 1)
                for q, (off, w, diag) in enumerate(ch):
                    scalar.wait_ge(S.h["tE"], tE_at[(mt, ci)] - nch[ci] + q + 1)
                    r0 = regions[(2 * q) % 4]
                    r1 = regions[(2 * q + 1) % 4]
                    bias = P.stat.ap()[:, 61:62] if diag else P.stat.ap()[:, 62:63]
                    if diag:
                        scalar.wait_ge(S.h["vE"], vE_at[(mt, ci)] - 2)
                    i = scalar.activation(
                        out=pstr[sl][:, off:off + w],
                        in_=r0[0].ap()[:, r0[1]:r0[1] + w],
                        func=AF.Exp, scale=0.125, bias=bias,
                        accum_out=lpt[:, q:q + 1])
                    S.inc(i, "sE", 1)
                    i = scalar.activation(
                        out=pstr[2 + sl][:, off:off + w],
                        in_=r1[0].ap()[:, r1[1]:r1[1] + w],
                        func=AF.Exp, scale=0.125, bias=bias,
                        accum_out=lpb[:, q:q + 1])
                    S.inc(i, "sE", 1)

    @blk.vector
    def _(vector):
        for mt in range(B_PAIRS):
            for ci in range(32):
                nq = nch[ci]
                sl = ci % 2
                vector.wait_ge(S.h["tE"], tE_at[(mt, ci)])
                r0 = regions[(2 * (nq - 1)) % 4]
                r1 = regions[(2 * (nq - 1) + 1) % 4]
                vector.scalar_tensor_tensor(
                    r0[0].ap()[:, r0[1]:r0[1] + 128],
                    r0[0].ap()[:, r0[1]:r0[1] + 128],
                    DC, P.trilf.ap()[:, :], ALU.add, ALU.mult)
                i = vector.scalar_tensor_tensor(
                    r1[0].ap()[:, r1[1]:r1[1] + 128],
                    r1[0].ap()[:, r1[1]:r1[1] + 128],
                    DC, P.trilf.ap()[:, :], ALU.add, ALU.mult)
                S.inc(i, "vE", 1)
                vector.wait_ge(S.h["sE"], sE_at[(mt, ci)])
                vector.reduce_sum(P.stat.ap()[:, 42:43], lpt[:, 0:nq], axis=AX.X)
                vector.reduce_sum(P.stat.ap()[:, 43:44], lpb[:, 0:nq], axis=AX.X)
                vector.drain()
                i = vector.reciprocal(P.stat.ap()[:, 44:46], P.stat.ap()[:, 42:44])
                S.inc(i, "vE", 1)
                vector.drain()
                vector.wait_ge(S.h["tE"], tE_at[(mt, ci)] + 1)
                vector.wait_ge(S.h["tE"], prev_tEz(mt, ci))
                vector.tensor_scalar(vts[:, sl, 0:64], P.PT.ap()[:, 0:64],
                                     P.stat.ap()[:, 44:45], None, ALU.mult)
                i = vector.tensor_scalar(vts[:, sl, 64:128], P.PT.ap()[:, 64:128],
                                         P.stat.ap()[:, 45:46], None, ALU.mult)
                S.inc(i, "vE", 1)
                if ci % 2 == 1:
                    g = ci // 2
                    e1 = ci * 128 + 128
                    ncc = (e1 + 511) // 512
                    for cc in range(ncc):
                        off = cc * 512
                        w = min(512, e1 - off)
                        pco = 512 * (cc % 2)
                        vector.wait_ge(S.h["tE"], tE_z[(mt, g)] - ncc + cc + 1)
                        split = min(max(off, 256 * g), off + w)
                        i = None
                        if split > off:
                            i = vector.tensor_tensor(
                                zacc[:, off:split], zacc[:, off:split],
                                P.PC.ap()[:, pco:pco + (split - off)], ALU.add)
                        if off + w > split:
                            i = vector.tensor_copy(
                                zacc[:, split:off + w],
                                P.PC.ap()[:, pco + (split - off):pco + w])
                        S.inc(i, "vE", 1)
            if mt > 0:
                vector.wait_ge(S.h["gp"], b_g + 16 * mt)
            vector.drain()
            i = vector.tensor_copy(zbf[:, :], zacc[:, :])
            S.inc(i, "vE", 1)

    @blk.gpsimd
    def _(g):
        for mt in range(B_PAIRS):
            g.wait_ge(S.h["vE"], vE_z[(mt, 15)] + 1)
            g.wait_ge(S.h["gp"], b_g + 16 * mt)
            i = g.dma_start(out=P.cin_z.ap()[ts(mt), :], in_=zbf[:, :])
            S.inc(i, "gp", 16)
        g.wait_ge(S.h["gp"], S.val("gp"))
        if B_CC:
            i = g.collective_compute(
                "AllGather", ALU.bypass,
                replica_groups=[[0, 1], [2, 3], [4, 5], [6, 7]],
                ins=[P.cin_z.ap().opt()], outs=[P.cout_z.ap().opt()])
            S.inc(i, "cc", 1)

    P.taps["cin_z"] = P.cin_z.ap()


def _phase_C(P):
    nc, S, blk, view = P.nc, P.S, P.blk, P.view
    A = P.A_tr
    h_sb = view(65536, [128, 4, TOK], BF16)
    ws1b = view(81920, [128, NKC, 512], BF16)
    ws2b = view(90112, [128, 4, D], BF16)
    w1r = [view(A + i * 8192, [128, NKC, 512], BF16) for i in range(2)]
    w2r = [view(A + 16384 + i * 8192, [128, 4, D], BF16) for i in range(2)]
    wrb = view(A + 32768, [128, NKC, 8], BF16)
    lg_sb = view(A + 36864, [128, NT, 8], F32)
    pr_sb = view(A + 39936, [128, NT, 8], F32)
    sc8 = view(A + 43008, [128, 8], F32)

    b_t, b_v, b_s, b_ld, b_w = (S.val("tE"), S.val("vE"), S.val("sE"),
                                S.val("ldA"), S.val("ldW"))

    ldw_base = b_w

    def ldw_w1(e):
        return ldw_base + 48 + 32 * e + 16

    def ldw_w2(e):
        return ldw_base + 48 + 32 * e + 32

    tE_rt = {tt: b_t + tt + 1 for tt in range(NT)}
    tE_h, tE_y = {}, {}
    t = b_t + NT
    for ee in range(9):
        t += 4
        tE_h[ee] = t
        t += NT
        tE_y[ee] = t
    P.C_tE_end = t

    vE_rt = {tt: b_v + 8 * tt + 8 for tt in range(NT)}
    v = b_v + 8 * NT
    vE_h, vE_y = {}, {}
    for ee in range(9):
        v += 4
        vE_h[ee] = v
        v += NT
        vE_y[ee] = v
    P.C_vE_end = v
    sE_rt = {tt: b_s + tt + 1 for tt in range(NT)}
    sE_h = {ee: b_s + NT + 4 * (ee + 1) for ee in range(9)}
    P.C_sE_end = b_s + NT + 36


    @blk.sync
    def _(sync):
        sync.wait_ge(S.h["tE"], b_t)
        sync.wait_ge(S.h["vE"], b_v)
        sync.wait_ge(S.h["ldA"], b_ld)
        i = sync.dma_start(
            out=P.x1res[:, :, :],
            in_=P.IN["xo"].rearrange("(tt p) d -> p tt d", p=128))
        S.inc(i, "ldA", 16)

    @blk.gpsimd
    def _(g):
        g.wait_ge(S.h["tE"], b_t)
        g.wait_ge(S.h["sE"], b_s)
        g.wait_ge(S.h["vE"], b_v)
        g.wait_ge(S.h["ldW"], b_w)
        i = g.dma_start(out=ws1b[:, :, :],
                        in_=P.IN["ws1"].rearrange("(k p) f -> p k f", p=128))
        S.inc(i, "ldW", 16)
        g.wait_ge(S.h["ldW"], S.val("ldW"))
        i = g.dma_start(out=ws2b[:, :, :],
                        in_=P.IN["ws2"].rearrange("(k p) d -> p k d", p=128))
        S.inc(i, "ldW", 16)
        g.wait_ge(S.h["ldW"], S.val("ldW"))
        i = g.dma_start(out=wrb[:, :, :],
                        in_=P.IN["wr"].rearrange("(k p) e -> p k e", p=128))
        S.inc(i, "ldW", 16)
        for e in range(8):
            g.wait_ge(S.h["ldW"], S.val("ldW"))
            if e >= 2:
                g.wait_ge(S.h["tE"], tE_h[e - 2 + 1])
            i = g.dma_start(out=w1r[e % 2][:, :, :],
                            in_=P.IN["w1"].rearrange("e (k p) f -> e p k f", p=128)[e])
            S.inc(i, "ldW", 16)
            g.wait_ge(S.h["ldW"], S.val("ldW"))
            if e >= 2:
                g.wait_ge(S.h["tE"], tE_y[e - 2 + 1])
            i = g.dma_start(out=w2r[e % 2][:, :, :],
                            in_=P.IN["w2"].rearrange("e (k p) d -> e p k d", p=128)[e])
            S.inc(i, "ldW", 16)
    regions4 = [(P.PA, 0), (P.PA, 512), (P.PB, 0), (P.PB, 512)]

    @blk.tensor
    def _(tensor):
        tensor.wait_ge(S.h["ldW"], ldw_base + 48)
        for tt in range(NT):
            last = None
            for k in range(NKC):
                last = tensor.matmul(P.PS.ap()[:, 0:8],
                                     P.xnof[:, k, ts(tt)], wrb[:, k, :],
                                     start=(k == 0), stop=(k == NKC - 1))
            S.inc(last, "tE", 1)
            tensor.wait_ge(S.h["vE"], b_v + 8 * tt + 1)
        for ee in range(9):
            e = ee - 1
            w1x = ws1b if ee == 0 else w1r[e % 2]
            w2x = ws2b if ee == 0 else w2r[e % 2]
            if ee > 0:
                tensor.wait_ge(S.h["ldW"], ldw_w1(e))
                tensor.wait_ge(S.h["vE"], vE_y[ee - 1])
            for fm in range(4):
                if ee > 0 or fm > 0:
                    pass
                for k in range(NKC):
                    for n in range(4):
                        pt, off = regions4[n]
                        last = tensor.matmul(
                            pt.ap()[:, off:off + 512],
                            w1x[:, k, ts(fm)],
                            P.xnof[:, k, ts(n, 512)],
                            start=(k == 0), stop=(k == NKC - 1))
                S.inc(last, "tE", 1)
                tensor.wait_ge(S.h["sE"], sE_h[ee] - 4 + fm + 1)
            if ee > 0:
                tensor.wait_ge(S.h["ldW"], ldw_w2(e))
            for tt in range(NT):
                ypt = P.PC if tt % 2 == 0 else P.PA
                if tt >= 2:
                    tensor.wait_ge(S.h["vE"], vE_y[ee] - NT + tt - 1)
                for kf in range(4):
                    for half in range(2):
                        last = tensor.matmul(
                            ypt.ap()[:, half * 512:half * 512 + 512],
                            h_sb[:, kf, ts(tt)],
                            w2x[:, kf, half * 512:half * 512 + 512],
                            start=(kf == 0), stop=(kf == 3))
                S.inc(last, "tE", 1)

    @blk.vector
    def _(vector):
        st = P.stat.ap()
        for tt in range(NT):
            vector.wait_ge(S.h["tE"], tE_rt[tt])
            i = vector.tensor_copy(lg_sb[:, tt, :], P.PS.ap()[:, 0:8])
            S.inc(i, "vE", 1)
            vector.drain()
            i = vector.reduce_max(st[:, 52:53], lg_sb[:, tt, :], axis=AX.X)
            S.inc(i, "vE", 1)
            vector.drain()
            i = vector.tensor_scalar(pr_sb[:, tt, :], lg_sb[:, tt, :],
                                     st[:, 52:53], None, ALU.is_equal)
            S.inc(i, "vE", 1)
            vector.drain()
            i = vector.scalar_tensor_tensor(sc8[:, :], pr_sb[:, tt, :], -3e38,
                                            lg_sb[:, tt, :], ALU.mult, ALU.add)
            S.inc(i, "vE", 1)
            vector.drain()
            i = vector.reduce_max(st[:, 53:54], sc8[:, :], axis=AX.X)
            S.inc(i, "vE", 1)
            vector.drain()
            i = vector.tensor_sub(st[:, 54:55], st[:, 52:53], st[:, 53:54])
            S.inc(i, "vE", 1)
            vector.drain()
            vector.wait_ge(S.h["sE"], sE_rt[tt])
            i = vector.tensor_scalar(st[:, 56:57], st[:, 55:56], 1.0, None,
                                     ALU.is_ge)
            S.inc(i, "vE", 1)
            vector.drain()
            i = vector.tensor_scalar(pr_sb[:, tt, :], pr_sb[:, tt, :],
                                     st[:, 56:57], None, ALU.mult)
            S.inc(i, "vE", 1)
        for ee in range(9):
            e = ee - 1
            for fm in range(4):
                vector.wait_ge(S.h["tE"], tE_h[ee] - 4 + fm + 1)
                last = None
                for n in range(4):
                    pt, off = regions4[n]
                    last = vector.tensor_scalar(
                        pt.ap()[:, off:off + 512], pt.ap()[:, off:off + 512],
                        0.0, None, ALU.max)
                S.inc(last, "vE", 1)
            for tt in range(NT):
                ypt = P.PC if tt % 2 == 0 else P.PA
                vector.wait_ge(S.h["tE"], tE_y[ee] - NT + tt + 1)
                if ee == 0 and tt == 0:
                    vector.wait_ge(S.h["ldA"], b_ld + 16)
                if ee == 0:
                    i = vector.scalar_tensor_tensor(
                        P.x1res[:, tt, :], ypt.ap()[:, :], 1.0,
                        P.x1res[:, tt, :], ALU.mult, ALU.add)
                else:
                    i = vector.scalar_tensor_tensor(
                        P.x1res[:, tt, :], ypt.ap()[:, :],
                        pr_sb[:, tt, e:e + 1], P.x1res[:, tt, :],
                        ALU.mult, ALU.add)
                S.inc(i, "vE", 1)

    @blk.scalar
    def _(scalar):
        st = P.stat.ap()
        for tt in range(NT):
            scalar.wait_ge(S.h["vE"], vE_rt[tt] - 2)
            i = scalar.activation(out=st[:, 55:56], in_=st[:, 54:55],
                                  func=AF.Sigmoid)
            S.inc(i, "sE", 1)
        for ee in range(9):
            for fm in range(4):
                scalar.wait_ge(S.h["vE"], vE_h[ee] - 4 + fm + 1)
                last = None
                for n in range(4):
                    pt, off = regions4[n]
                    last = scalar.activation(
                        out=h_sb[:, fm, ts(n, 512)],
                        in_=pt.ap()[:, off:off + 512], func=AF.Square)
                S.inc(last, "sE", 1)

    P.taps["x1res_C"] = P.x1res[:, :, :]


def _phase_D(P):
    nc, S, blk, view = P.nc, P.S, P.blk, P.view
    wob = view(65536, [128, NKC, D], BF16)
    zall = view(81920, [128, NKC, TOK], BF16)

    b_t, b_v, b_s, b_w = S.val("tE"), S.val("vE"), S.val("sE"), S.val("ldW")
    C_tE_end, C_vE_end, C_sE_end = P.C_tE_end, P.C_vE_end, P.C_sE_end

    @blk.gpsimd
    def _(g):
        g.wait_ge(S.h["ldW"], b_w)
        g.wait_ge(S.h["tE"], C_tE_end)
        i = g.dma_start(out=wob[:, :, :],
                        in_=P.IN["wo"].rearrange("(k p) e -> p k e", p=128))
        S.inc(i, "ldW", 16)
        g.wait_ge(S.h["cc"], 1)
        pid = g.partition_id()
        r = g.alloc_register("zoff")
        g.reg_alu(r, pid, 1, ALU.bitwise_and)
        g.reg_alu(r, r, 11, ALU.logical_shift_left)
        for j in range(2):
            g.wait_ge(S.h["ldW"], S.val("ldW"))
            base = P.cout_z.ap()[j].rearrange("(k p) c -> p k c", p=128)[:, :, 0:TOK]
            r2 = g.alloc_register(f"zoff{j}")
            g.reg_alu(r2, r, j * 512 * T, ALU.add)
            src = dataclasses.replace(base, offset=g.snap(r2, min_val=0,
                                                          max_val=2 * 512 * T))
            i = g.dma_start(out=zall[:, 4 * j:4 * (j + 1), :], in_=src)
            S.inc(i, "ldW", 16)

    tE_wo = {tt: b_t + tt + 1 for tt in range(NT)}
    P.D_tE_end = b_t + NT
    vE_wo = {tt: b_v + tt + 1 for tt in range(NT)}
    P.D_vE_end = b_v + NT

    @blk.tensor
    def _(tensor):
        tensor.wait_ge(S.h["ldW"], b_w + 48)
        for tt in range(NT):
            ypt = P.PC if tt % 2 == 0 else P.PA
            if tt >= 2:
                tensor.wait_ge(S.h["vE"], vE_wo[tt - 2])
            else:
                tensor.wait_ge(S.h["vE"], C_vE_end)
            for k in range(NKC):
                for half in range(2):
                    last = tensor.matmul(
                        ypt.ap()[:, half * 512:half * 512 + 512],
                        zall[:, k, ts(tt)],
                        wob[:, k, half * 512:half * 512 + 512],
                        start=(k == 0), stop=(k == NKC - 1))
            S.inc(last, "tE", 1)

    @blk.vector
    def _(vector):
        for tt in range(NT):
            ypt = P.PC if tt % 2 == 0 else P.PA
            vector.wait_ge(S.h["tE"], tE_wo[tt])
            i = vector.tensor_tensor(P.x1res[:, tt, :], P.x1res[:, tt, :],
                                     ypt.ap()[:, :], ALU.add)
            S.inc(i, "vE", 1)

    P.taps["x1res_D"] = P.x1res[:, :, :]


def _phase_D2(P):
    nc, S, blk, view = P.nc, P.S, P.blk, P.view
    xnt = [view(65536 + i * 2048, [128, 1024], BF16) for i in range(2)]
    st = P.stat.ap()

    b_t, b_v, b_s, b_g = S.val("tE"), S.val("vE"), S.val("sE"), S.val("gp")
    V_TOT, S_TOT, T_TOT = 2, 4, 1

    @blk.vector
    def _(vector):
        inv = 1.0 / D
        for tt in range(NT):
            if tt == 0:
                vector.wait_ge(S.h["vE"], b_v)
            vector.reduce_sum(st[:, 0:1], P.x1res[:, tt, :], axis=AX.X)
            vector.wait_ge(S.h["sE"], b_s + tt * S_TOT + 1)
            vector.drain()
            vector.tensor_scalar(st[:, 8:9], st[:, 0:1], inv, None, ALU.mult)
            vector.tensor_scalar(st[:, 12:13], st[:, 4:5], inv, None, ALU.mult)
            vector.drain()
            vector.tensor_tensor(st[:, 20:21], st[:, 8:9], st[:, 8:9], ALU.mult)
            vector.drain()
            vector.tensor_sub(st[:, 12:13], st[:, 12:13], st[:, 20:21])
            vector.drain()
            vector.tensor_scalar(st[:, 12:13], st[:, 12:13], 1e-5, None, ALU.add)
            vector.drain()
            i = vector.reciprocal(st[:, 12:13], st[:, 12:13])
            S.inc(i, "vE", 1)
            vector.wait_ge(S.h["sE"], b_s + tt * S_TOT + 2)
            if tt >= 2:
                vector.wait_ge(S.h["tE"], b_t + (tt - 2) * T_TOT + 1)
            vector.tensor_copy(P.mst.ap()[:, tt, 0:1], st[:, 8:9])
            vector.tensor_copy(P.mst.ap()[:, tt, 1:2], st[:, 16:17])
            i = vector.tensor_scalar(xnt[tt % 2][:, :], P.x1res[:, tt, :],
                                     st[:, 8:9], st[:, 16:17],
                                     ALU.subtract, ALU.mult)
            S.inc(i, "vE", 1)

    @blk.scalar
    def _(scalar):
        for tt in range(NT):
            if tt == 0:
                scalar.wait_ge(S.h["vE"], b_v)
            i = scalar.activation(out=P.trash.ap()[:, 0, :],
                                  in_=P.x1res[:, tt, :],
                                  func=AF.Square, accum_out=st[:, 4:5])
            S.inc(i, "sE", 1)
            scalar.wait_ge(S.h["vE"], b_v + tt * V_TOT + 1)
            i = scalar.activation(out=st[:, 16:17], in_=st[:, 12:13], func=AF.Sqrt)
            S.inc(i, "sE", 1)
            scalar.wait_ge(S.h["vE"], b_v + tt * V_TOT + 2)
            i = scalar.activation(out=P.trash.ap()[:, 1, :],
                                  in_=xnt[tt % 2][:, :],
                                  func=AF.Square, accum_out=P.xsq.ap()[:, tt:tt + 1])
            S.inc(i, "sE", 1)
            scalar.wait_ge(S.h["tE"], b_t + tt * T_TOT + 1)
            i = scalar.copy(
                P.xnof[:, :, ts(tt)],
                P.PT.ap()[:, :].rearrange("p (a b) -> p a b", a=NKC, b=128))
            S.inc(i, "sE", 1)

    @blk.tensor
    def _(tensor):
        for tt in range(NT):
            tensor.wait_ge(S.h["vE"], b_v + tt * V_TOT + 2)
            if tt >= 1:
                tensor.wait_ge(S.h["sE"], b_s + (tt - 1) * S_TOT + 4)
            last = None
            for k in range(NKC):
                last = tensor.transpose(P.PT.ap()[:, ts(k)],
                                        xnt[tt % 2][:, ts(k)],
                                        P.eyeb.ap()[:, :])
            S.inc(last, "tE", 1)

    @blk.gpsimd
    def _(g):
        g.wait_ge(S.h["sE"], b_s + NT * S_TOT)
        g.wait_ge(S.h["gp"], b_g)
        i = g.dma_start(
            out=P.cin_xt.ap().rearrange("(kc p) t -> p kc t", p=128),
            in_=P.xnof[:, :, :])
        S.inc(i, "gp", 16)
        g.wait_ge(S.h["gp"], S.val("gp"))
        i = g.collective_compute(
            "AllGather", ALU.bypass,
            replica_groups=[[0, 1], [2, 3], [4, 5], [6, 7]],
            ins=[P.cin_xt.ap().opt()], outs=[P.cout_xt.ap().opt()])
        S.inc(i, "cc", 1)

    P.D2_tE_end = S.val("tE")
    P.taps["cin_xt"] = P.cin_xt.ap()


def _phase_F(P):
    nc, S, blk, view = P.nc, P.S, P.blk, P.view
    A = P.A_tr
    w1r = [view(65536 + i * 8192, [128, NKC, 512], BF16) for i in range(2)]
    xnr = [view(81920 + i * 4096, [128, 1024], F32) for i in range(2)]
    xnr_t = view(90112, [128, NKC, 128], F32)
    h1h = view(A, [128, 16, 1024], BF16)
    w2h = view(A + 32768, [128, 16, 1024], BF16)
    st = P.stat.ap()
    F32R = mybir.dt.float32r

    b_t, b_v, b_s, b_w, b_g = (S.val("tE"), S.val("vE"), S.val("sE"),
                               S.val("ldW"), S.val("gp"))

    PREP_V = 5
    PREP_T = 1
    tE_tr = {}
    tE_lg = {}
    tE_t2a = {}
    tE_t2 = {}
    t = b_t + PREP_T
    for tt in range(NT):
        for k in range(NKC):
            t += 1
            tE_tr[(tt, k)] = t
        t += 1
        tE_lg[tt] = t
        t += 1
        tE_t2a[tt] = t
        t += 1
        tE_t2[tt] = t
    tE_h, tE_y = {}, {}
    for th in range(2):
        for e in range(16):
            for fh in range(2):
                t += 16
                tE_h[(th, e, fh)] = t
                t += 8
                tE_y[(th, e, fh)] = t
    P.F_tE_end = t

    vE_ap, vE_lg, vE_omg, vE_t2, vE_y = {}, {}, {}, {}, {}
    v = b_v + PREP_V
    for tt in range(NT):
        v += 1
        vE_ap[tt] = v
        v += 1
        vE_lg[tt] = v
        v += 17
        vE_omg[tt] = v
        v += 1
        v += 1
        vE_t2[tt] = v
    for th in range(2):
        for e in range(16):
            for fh in range(2):
                v += 8
                vE_y[(th, e, fh)] = v
    P.F_vE_end = v

    sE_xsq, sE_ev, sE_exp, sE_wtt = {}, {}, {}, {}
    sv = b_s
    for tt in range(NT):
        sv += 1
        sE_xsq[tt] = sv
        for k in range(NKC):
            sv += 1
            sE_ev[(tt, k)] = sv
        sv += 1
        sE_exp[tt] = sv
        sv += 1
        sE_wtt[tt] = sv
    sE_h = {}
    for th in range(2):
        for e in range(16):
            for fh in range(2):
                sv += 16
                sE_h[(th, e, fh)] = sv
    P.F_sE_end = sv

    ldw0 = b_w + 96

    def ldw_w1(th, e, fh, mg):
        idx = ((th * 16 + e) * 2 + fh)
        return ldw0 + idx * 80 + (mg + 1) * 16

    def ldw_w2(th, e, fh):
        idx = ((th * 16 + e) * 2 + fh)
        return ldw0 + idx * 80 + 80

    def prev_key(th, e, fh):
        return (th, e, fh - 1) if fh else (
            (th, e - 1, 1) if e else (th - 1, 15, 1))

    @blk.gpsimd
    def _(g):
        g.wait_ge(S.h["ldW"], b_w)
        g.wait_ge(S.h["tE"], b_t)
        krw2 = P.IN["krw"].rearrange("w (k p) e -> w p k e", p=128)
        i = g.dma_start(out=P.krwf.ap()[:, :, 0:16], in_=krw2[0])
        S.inc(i, "ldW", 16)
        g.wait_ge(S.h["ldW"], S.val("ldW"))
        i = g.dma_start(out=P.krwf.ap()[:, :, 16:32], in_=krw2[1])
        S.inc(i, "ldW", 16)
        g.wait_ge(S.h["ldW"], S.val("ldW"))
        i = g.dma_start(out=P.b1s.ap()[:, :, :], in_=P.IN["b1s"])
        S.inc(i, "ldW", 16)
        g.wait_ge(S.h["ldW"], S.val("ldW"))
        i = g.dma_start(out=P.b2s_v[0:16, :], in_=P.IN["b2s"])
        S.inc(i, "ldW", 16)
        g.wait_ge(S.h["ldW"], S.val("ldW"))
        i = g.dma_start(out=P.rbf.ap()[:, :], in_=P.IN["rbv"])
        S.inc(i, "ldW", 16)
        g.wait_ge(S.h["ldW"], S.val("ldW"))
        i = g.dma_start(out=P.eyer.ap()[:, :], in_=P.IN["eyer"])
        S.inc(i, "ldW", 16)
        for th in range(2):
            for e in range(16):
                for fh in range(2):
                    f0 = fh * 2048
                    for mg in range(4):
                        g.wait_ge(S.h["ldW"], S.val("ldW"))
                        if mg >= 2:
                            g.wait_ge(S.h["tE"],
                                      tE_h[(th, e, fh)] - 16 + (mg - 1) * 4)
                        elif (th, e, fh) != (0, 0, 0):
                            g.wait_ge(S.h["tE"], tE_h[prev_key(th, e, fh)])
                        i = g.dma_start(
                            out=w1r[mg % 2][:, :, :],
                            in_=P.IN["w1t"].rearrange(
                                "e (k p) f -> e p k f", p=128)[e][:, :, f0 + mg * 512: f0 + (mg + 1) * 512])
                        S.inc(i, "ldW", 16)
                    g.wait_ge(S.h["ldW"], S.val("ldW"))
                    if (th, e, fh) != (0, 0, 0):
                        g.wait_ge(S.h["tE"], tE_y[prev_key(th, e, fh)])
                    i = g.dma_start(
                        out=w2h[:, :, :],
                        in_=P.IN["w2t"].rearrange(
                            "e (k p) d -> e p k d", p=128)[e][:, fh * 16:(fh + 1) * 16, :])
                    S.inc(i, "ldW", 16)

    PSr = [P.PS.ap()[:, 0:128], P.PS.ap()[:, 128:256]]
    PSlg = P.PS.ap()[:, 384:400]
    PSk = P.PS.ap()[:, 400:416]

    @blk.tensor
    def _(tensor):
        tensor.wait_ge(S.h["sE"], b_s)
        tensor.wait_ge(S.h["ldW"], ldw0)
        tensor.wait_ge(S.h["vE"], b_v + 3)
        for k in range(NKC):
            last = tensor.matmul(PSk[0:1, :], P.onec.ap()[:, :],
                                 P.ksqf.ap()[:, k, :],
                                 start=(k == 0), stop=(k == NKC - 1))
        S.inc(last, "tE", 1)
        for tt in range(NT):
            tensor.wait_ge(S.h["vE"], vE_ap[tt])
            for k in range(NKC):
                if k >= 1:
                    tensor.wait_ge(S.h["sE"], sE_ev[(tt, k - 1)])
                elif tt >= 1:
                    tensor.wait_ge(S.h["sE"], sE_ev[(tt - 1, NKC - 1)])
                    tensor.wait_ge(S.h["vE"], vE_lg[tt - 1])
                i = tensor.transpose(PSr[k % 2], xnr[tt % 2][:, ts(k)],
                                     P.eyer.ap()[:, :])
                S.inc(i, "tE", 1)
            tensor.wait_ge(S.h["sE"], sE_ev[(tt, NKC - 1)])
            tensor.wait_ge(S.h["vE"], b_v + PREP_V)
            for k in range(NKC):
                tensor.matmul(PSlg, xnr_t[:, k, :], P.cmbr.ap()[:, k, :],
                              start=(k == 0), stop=False)
            last = tensor.matmul(PSlg, P.oner.ap()[:, :], P.rbk.ap()[:, :],
                                 start=False, stop=True)
            S.inc(last, "tE", 1)
            tensor.wait_ge(S.h["vE"], vE_omg[tt] + 1)
            i = tensor.transpose(P.PT.ap()[0:16, 0:128], P.wtb.ap()[:, :],
                                 P.eyeb.ap()[:, :])
            S.inc(i, "tE", 1)
            tensor.wait_ge(S.h["sE"], sE_wtt[tt])
            for half in range(2):
                last = tensor.matmul(
                    P.PC.ap()[:, half * 512:half * 512 + 512],
                    P.wtt.ap()[:, :],
                    P.b2s_v[0:16, half * 512:half * 512 + 512],
                    start=True, stop=True)
            S.inc(last, "tE", 1)
            tensor.wait_ge(S.h["vE"], vE_t2[tt])
        for th in range(2):
            for e in range(16):
                for fh in range(2):
                    for mc in range(16):
                        pt = P.PA if mc % 2 == 0 else P.PB
                        mg = mc // 4
                        tensor.wait_ge(S.h["ldW"], ldw_w1(th, e, fh, mg))
                        if mc >= 2:
                            tensor.wait_ge(
                                S.h["sE"], sE_h[(th, e, fh)] - 16 + mc - 1)
                        elif (th, e, fh) != (0, 0, 0):
                            prev = prev_key(th, e, fh)
                            tensor.wait_ge(S.h["sE"], sE_h[prev])
                            tensor.wait_ge(S.h["vE"], vE_y[prev])
                        else:
                            tensor.wait_ge(S.h["vE"], vE_t2[NT - 1])
                        for k in range(NKC):
                            for half in range(2):
                                last = tensor.matmul(
                                    pt.ap()[:, half * 512:half * 512 + 512],
                                    w1r[mg % 2][:, k, ts(mc % 4)],
                                    P.xnof[:, k, th * 1024 + half * 512:
                                           th * 1024 + half * 512 + 512],
                                    start=(k == 0), stop=(k == NKC - 1))
                        S.inc(last, "tE", 1)
                    tensor.wait_ge(S.h["ldW"], ldw_w2(th, e, fh))
                    tensor.wait_ge(S.h["sE"], sE_h[(th, e, fh)])
                    for tq in range(8):
                        ypt = P.PC if tq % 2 == 0 else P.PA
                        if tq >= 2:
                            tensor.wait_ge(
                                S.h["vE"], vE_y[(th, e, fh)] - 8 + tq - 1)
                        for mc in range(16):
                            for half in range(2):
                                last = tensor.matmul(
                                    ypt.ap()[:, half * 512:half * 512 + 512],
                                    h1h[:, mc, ts(tq)],
                                    w2h[:, mc, half * 512:half * 512 + 512],
                                    start=(mc == 0), stop=(mc == 15))
                        S.inc(last, "tE", 1)

    vc = [b_v]

    def vinc(i):
        S.inc(i, "vE", 1)
        vc[0] += 1
        return vc[0]

    @blk.vector
    def _(vector):
        vector.wait_ge(S.h["ldW"], ldw0)
        last = None
        for k in range(NKC):
            last = vector.scalar_tensor_tensor(
                P.cmbr.ap()[:, k, :], P.krwf.ap()[:, k, 0:16], 2.0,
                P.krwf.ap()[:, k, 16:32], ALU.mult, ALU.add)
        vinc(last)
        i = vector.tensor_tensor(P.ksqf.ap()[:, :, :], P.krwf.ap()[:, :, 0:16],
                                 P.krwf.ap()[:, :, 0:16], ALU.mult)
        vinc(i)
        vector.memset(P.onec.ap()[:, :], 1.0)
        i = vector.memset(P.oner.ap()[:, :], 1.0)
        vinc(i)
        vector.wait_ge(S.h["tE"], b_t + PREP_T)
        i = vector.tensor_sub(P.rbf.ap()[:, :], P.rbf.ap()[:, :], PSk[0:1, :])
        vinc(i)
        vector.drain()
        i = vector.tensor_copy(P.rbk.ap()[:, :], P.rbf.ap()[:, :])
        vinc(i)
        assert vc[0] == b_v + PREP_V
        for tt in range(NT):
            if tt >= 2:
                vector.wait_ge(S.h["tE"], tE_lg[tt - 2])
                vector.wait_ge(S.h["sE"], sE_xsq[tt - 2])
            i = vector.tensor_scalar(xnr[tt % 2][:, :], P.x1res[:, tt, :],
                                     P.mst.ap()[:, tt, 0:1],
                                     P.mst.ap()[:, tt, 1:2],
                                     ALU.subtract, ALU.mult)
            vinc(i)
            assert vc[0] == vE_ap[tt]
            vector.wait_ge(S.h["tE"], tE_lg[tt])
            vector.wait_ge(S.h["sE"], sE_xsq[tt])
            i = vector.tensor_scalar(P.lg16.ap()[:, :], PSlg,
                                     P.xsq.ap()[:, tt:tt + 1], None,
                                     ALU.subtract)
            vinc(i)
            assert vc[0] == vE_lg[tt]
            vector.drain()
            i = vector.tensor_copy(P.cur16.ap()[:, :], P.lg16.ap()[:, :])
            vinc(i)
            vector.drain()
            for it in range(3):
                dst = st[:, 52:53] if it == 0 else st[:, 53:54]
                i = vector.reduce_max(dst, P.cur16.ap()[:, :], axis=AX.X)
                vinc(i)
                vector.drain()
                i = vector.tensor_scalar(P.eq16.ap()[:, :], P.cur16.ap()[:, :],
                                         dst, None, ALU.is_equal)
                vinc(i)
                vector.drain()
                i = vector.scalar_tensor_tensor(P.cur16.ap()[:, :],
                                                P.eq16.ap()[:, :], -3e38,
                                                P.cur16.ap()[:, :],
                                                ALU.mult, ALU.add)
                vinc(i)
                vector.drain()
            i = vector.reduce_max(st[:, 54:55], P.cur16.ap()[:, :], axis=AX.X)
            vinc(i)
            vector.drain()
            i = vector.tensor_scalar(P.eq16.ap()[:, :], P.lg16.ap()[:, :],
                                     st[:, 54:55], None, ALU.is_ge)
            vinc(i)
            i = vector.tensor_scalar(st[:, 55:56], st[:, 52:53], -1.0, None,
                                     ALU.mult)
            vinc(i)
            vector.drain()
            vector.wait_ge(S.h["sE"], sE_exp[tt])
            i = vector.tensor_tensor(P.lg16.ap()[:, :], P.lg16.ap()[:, :],
                                     P.eq16.ap()[:, :], ALU.mult)
            vinc(i)
            vector.drain()
            i = vector.reduce_sum(st[:, 56:57], P.lg16.ap()[:, :], axis=AX.X)
            vinc(i)
            vector.drain()
            i = vector.reciprocal(st[:, 57:58], st[:, 56:57])
            vinc(i)
            vector.drain()
            i = vector.tensor_scalar(P.omg.ap()[:, tt, :], P.lg16.ap()[:, :],
                                     st[:, 57:58], None, ALU.mult)
            vinc(i)
            vector.drain()
            i = vector.tensor_copy(P.wtb.ap()[:, :], P.omg.ap()[:, tt, :])
            vinc(i)
            assert vc[0] == vE_omg[tt] + 1, (vc[0], vE_omg[tt], tt)
            vector.wait_ge(S.h["tE"], tE_t2[tt])
            i = vector.tensor_tensor(P.x1res[:, tt, :], P.x1res[:, tt, :],
                                     P.PC.ap()[:, :], ALU.add)
            vinc(i)
            assert vc[0] == vE_t2[tt]
        for th in range(2):
            for e in range(16):
                for fh in range(2):
                    for tq in range(8):
                        ypt = P.PC if tq % 2 == 0 else P.PA
                        vector.wait_ge(S.h["tE"], tE_y[(th, e, fh)] - 8 + tq + 1)
                        tt = th * 8 + tq
                        i = vector.scalar_tensor_tensor(
                            P.x1res[:, tt, :], ypt.ap()[:, :],
                            P.omg.ap()[:, tt, e:e + 1], P.x1res[:, tt, :],
                            ALU.mult, ALU.add)
                        vinc(i)
                    assert vc[0] == vE_y[(th, e, fh)]

    sc_ = [b_s]

    def sinc(i):
        S.inc(i, "sE", 1)
        sc_[0] += 1
        return sc_[0]

    @blk.scalar
    def _(scalar):
        for tt in range(NT):
            scalar.wait_ge(S.h["vE"], vE_ap[tt])
            i = scalar.activation(out=P.trash.ap()[:, 0, :],
                                  in_=xnr[tt % 2][:, :],
                                  func=AF.Square,
                                  accum_out=P.xsq.ap()[:, tt:tt + 1])
            sinc(i)
            assert sc_[0] == sE_xsq[tt]
            for k in range(NKC):
                scalar.wait_ge(S.h["tE"], tE_tr[(tt, k)])
                i = scalar.copy(xnr_t[:, k, :], PSr[k % 2])
                sinc(i)
            scalar.wait_ge(S.h["vE"], vE_omg[tt] - 4)
            i = scalar.activation(out=P.lg16.ap()[:, :], in_=P.lg16.ap()[:, :],
                                  func=AF.Exp, bias=st[:, 55:56])
            sinc(i)
            assert sc_[0] == sE_exp[tt]
            scalar.wait_ge(S.h["tE"], tE_t2a[tt])
            i = scalar.copy(P.wtt.ap()[:, :], P.PT.ap()[0:16, 0:128])
            sinc(i)
            assert sc_[0] == sE_wtt[tt]
        for th in range(2):
            for e in range(16):
                for fh in range(2):
                    for mc in range(16):
                        pt = P.PA if mc % 2 == 0 else P.PB
                        scalar.wait_ge(
                            S.h["tE"], tE_h[(th, e, fh)] - 16 + mc + 1)
                        i = scalar.activation(
                            out=h1h[:, mc, :], in_=pt.ap()[:, :],
                            func=AF.Gelu,
                            bias=P.b1s.ap()[:, e, fh * 16 + mc:fh * 16 + mc + 1])
                        sinc(i)
                    assert sc_[0] == sE_h[(th, e, fh)]

    P.taps["x1res_F"] = P.x1res[:, :, :]
    P.taps["omg"] = P.omg.ap()[:, :, :]


def _phase_E(P):
    nc, S, blk, view = P.nc, P.S, P.blk, P.view
    K_all = view(65536, [128, NKC, T], BF16)
    Q_all = view(131072, [128, NKC, TOK], BF16)
    W0 = 163840
    xh = view(W0, [128, T], BF16)
    xh_own = view(W0 + 8192, [128, TOK], BF16)
    rkh = view(W0 + 12288, [128, 2, 2, 512], BF16)
    kst = view(W0 + 16384, [128, 2, 512], BF16)
    qst = view(W0 + 18432, [128, 2, 512], BF16)
    vst = view(W0 + 20480, [128, 2, 512], BF16)
    vtst = view(W0 + 22528, [128, 16, 128], BF16)
    t1f = view(W0 + 26624, [128, 512], F32)
    mabs = view(W0 + 28672, [128, L1_NPAIRS, 2], F32)
    pstr = view(W0, [128, 4224], BF16)
    vtr = [view(W0 + 8448 + i * 8192, [128, 4, 1024], BF16) for i in range(2)]
    ptb = view(W0 + 24832, [128, 2, 128], BF16)
    msk = view(W0 + 25344, [128, 512], F32)
    lp2 = P.stat.ap()[:, 24:33]
    st = P.stat.ap()
    GAM = 1.0 / 32.0
    CBIG = 32768.0

    b_t, b_v, b_s, b_g, b_ld = (S.val("tE"), S.val("vE"), S.val("sE"),
                                S.val("gp"), S.val("ldA"))

    midx = {}
    col = 0
    for i in range(NT):
        for j in range(L1_EXT[i]):
            midx[(i, j)] = col
            col += 1

    def chunks_of(i):
        W = L1_EXT[i] * 128
        out, off = [], 0
        while off < W:
            w = min(512, W - off)
            out.append((off, w))
            off += w
        return out

    tE1_kp, tE1_ks, tE1_qp, tE1_qs, tE1_vp, tE1_vt = ({} for _ in range(6))
    t = b_t
    for hp in range(8):
        for c in range(NKC):
            t += 1
            tE1_kp[(hp, c)] = t
            t += 1
            tE1_ks[(hp, c)] = t
        for c in range(4):
            t += 1
            tE1_qp[(hp, c)] = t
            t += 1
            tE1_qs[(hp, c)] = t
        for c in range(NKC):
            t += 1
            tE1_vp[(hp, c)] = t
            if c >= 1:
                for j in range(4 * (c - 1), 4 * c):
                    t += 1
                    tE1_vt[(hp, j)] = t
        for j in range(28, 32):
            t += 1
            tE1_vt[(hp, j)] = t
    E1_tE_end = t

    sE1_ke, sE1_qe, sE1_ve, sE1_vte = ({} for _ in range(4))
    sv = b_s
    for hp in range(8):
        for c in range(NKC):
            sv += 1
            sE1_ke[(hp, c)] = sv
        for c in range(4):
            sv += 1
            sE1_qe[(hp, c)] = sv
        for c in range(NKC):
            sv += 1
            sE1_ve[(hp, c)] = sv
            for j in range(4 * c, 4 * c + 4):
                sv += 1
                sE1_vte[(hp, j)] = sv
    E1_sE_end = sv

    vE1_k, vE1_q = {}, {}
    v = b_v
    for hp in range(8):
        for c in range(NKC):
            v += 1
            vE1_k[(hp, c)] = v
        for c in range(4):
            v += 1
            vE1_q[(hp, c)] = v
    E1_vE_end = v

    gp_xh, gp_xho, gp_vo = ({} for _ in range(3))
    g_ = b_g
    for hp in range(8):
        for h in range(2):
            g_ += 32
            gp_xh[(hp, h)] = g_
        g_ += 32
        gp_xho[hp] = g_
        for h2 in range(2):
            g_ += 16
            gp_vo[(hp, h2)] = g_
    E1_gp_end = g_
    ld_kt, ld_qt = {}, {}
    lv = b_ld
    for hp in range(8):
        for c in range(NKC):
            lv += 16
            ld_kt[(hp, c)] = lv
        for c in range(4):
            lv += 16
            ld_qt[(hp, c)] = lv
    E1_ld_end = lv

    tE2_s, tE2_pt, tE2_y, vE2_m, vE2_l, vE2_y, sE2_e, sE2_pe = ({} for _ in range(8))
    ld2 = {}
    t2, v2, s2, l2 = E1_tE_end, E1_vE_end, E1_sE_end, E1_ld_end
    for i in range(NT):
        ch = chunks_of(i)
        for q in range(len(ch)):
            v2 += 1
            vE2_m[(i, q)] = v2
            t2 += 1
            tE2_s[(i, q)] = t2
            s2 += 1
            sE2_e[(i, q)] = s2
        v2 += 1
        vE2_l[i] = v2
        for kt in range(L1_EXT[i]):
            t2 += 1
            tE2_pt[(i, kt)] = t2
            s2 += 1
            sE2_pe[(i, kt)] = s2
            t2 += 1
            tE2_y[(i, kt)] = t2
        v2 += 1
        vE2_y[i] = v2
        for g2 in range((L1_EXT[i] + 3) // 4):
            l2 += 16
            ld2[(i, g2)] = l2
    P.E_tE_end, P.E_vE_end, P.E_sE_end = t2, v2, s2

    def grp_last_kt(i, g2):
        return min(g2 * 4 + 3, L1_EXT[i] - 1)

    @blk.gpsimd
    def _(g):
        g.wait_ge(S.h["cc"], 2)
        g.wait_ge(S.h["tE"], b_t)
        g.wait_ge(S.h["sE"], b_s)
        g.wait_ge(S.h["vE"], b_v)
        g.wait_ge(S.h["ldW"], S.val("ldW"))
        i = g.dma_start(out=P.lwb.ap()[:, :, :, :], in_=P.IN["l1wd"])
        S.inc(i, "ldW", 16)
        g.wait_ge(S.h["ldW"], S.val("ldW"))
        i = g.dma_start(out=P.swpb.ap()[:, :], in_=P.IN["swp"])
        S.inc(i, "ldW", 16)
        g.wait_ge(S.h["ldW"], S.val("ldW"))
        i = g.dma_start(out=mabs[:, :, :], in_=P.IN["mab"])
        S.inc(i, "ldW", 16)
        P.E_ldW_end = S.val("ldW")
        pid = g.partition_id()
        r = g.alloc_register("xtoff")
        g.reg_alu(r, pid, 1, ALU.bitwise_and)
        g.reg_alu(r, r, 21, ALU.logical_shift_left)
        r2 = g.alloc_register("xtoff2")
        for hp in range(8):
            if hp > 0:
                g.wait_ge(S.h["tE"], tE1_vp[(hp - 1, NKC - 1)])
            for h in range(2):
                src = P.cout_xt.ap()[h].rearrange("(q s) t -> s q t", q=64, s=16)
                for par in range(2):
                    g.wait_ge(S.h["gp"], S.val("gp"))
                    i = g.dma_start(out=xh[64 * par:64 * par + 64, ts(h, TOK)],
                                    in_=src[2 * hp + par])
                    S.inc(i, "gp", 16)
            base = P.cout_xt.ap()[0].rearrange("(q s) t -> s q t", q=64, s=16)
            for par in range(2):
                g.reg_alu(r2, r, (2 * hp + par) * TOK, ALU.add)
                src = dataclasses.replace(
                    base[2 * hp + par],
                    offset=g.snap(r2, min_val=0, max_val=2 * D * TOK))
                g.wait_ge(S.h["gp"], S.val("gp"))
                i = g.dma_start(out=xh_own[64 * par:64 * par + 64, :], in_=src)
                S.inc(i, "gp", 16)
            for h2 in range(2):
                g.wait_ge(S.h["sE"], sE1_vte[(hp, h2 * 16 + 15)])
                g.wait_ge(S.h["gp"], S.val("gp"))
                dst = P.vt_hbm.ap().rearrange("(tt p) f -> tt p f", p=128)
                i = g.dma_start(
                    out=dst[h2 * 16:(h2 + 1) * 16].transpose([1, 0, 2])[:, :, ts(hp)],
                    in_=vtst[:, :, :])
                S.inc(i, "gp", 16)

    @blk.sync
    def _(sync):
        sync.wait_ge(S.h["tE"], b_t)
        for hp in range(8):
            for c in range(NKC):
                if c >= 2:
                    sync.wait_ge(S.h["vE"], vE1_k[(hp, c - 2)])
                elif hp > 0:
                    sync.wait_ge(S.h["vE"], vE1_q[(hp - 1, 2 + c)])
                sync.wait_ge(S.h["ldA"], S.val("ldA"))
                i = sync.dma_start(
                    out=rkh[:, c % 2, :, :],
                    in_=P.IN["ropek"][hp][:, :, c * 512:(c + 1) * 512]
                    .rearrange("c p t -> p c t"))
                S.inc(i, "ldA", 16)
            for c in range(4):
                if c >= 2:
                    sync.wait_ge(S.h["vE"], vE1_q[(hp, c - 2)])
                else:
                    sync.wait_ge(S.h["vE"], vE1_k[(hp, 6 + c)])
                sync.wait_ge(S.h["ldA"], S.val("ldA"))
                i = sync.dma_start(
                    out=rkh[:, c % 2, :, :],
                    in_=P.IN["ropeq"][hp][:, :, c * 512:(c + 1) * 512]
                    .rearrange("c p t -> p c t"))
                S.inc(i, "ldA", 16)

    @blk.tensor
    def _(tensor):
        tensor.wait_ge(S.h["ldW"], P.E_ldW_end)

        def proj(w, hp, src, c):
            last = None
            for half in range(2):
                last = tensor.matmul(
                    P.PA.ap()[64 * half:64 * half + 64, 0:512],
                    P.lwb.ap()[64 * half:64 * half + 64, w, hp, :],
                    src[64 * half:64 * half + 64, ts(c, 512)],
                    start=True, stop=True,
                    tile_position=(64 * half, 64 * half))
            S.inc(last, "tE", 1)

        for hp in range(8):
            for c in range(NKC):
                tensor.wait_ge(S.h["gp"], gp_xh[(hp, c // 4)])
                if c > 0:
                    tensor.wait_ge(S.h["sE"], sE1_ke[(hp, c - 1)])
                elif hp > 0:
                    tensor.wait_ge(S.h["sE"], sE1_ve[(hp - 1, NKC - 1)])
                proj(1, hp, xh, c)
                tensor.wait_ge(S.h["sE"], sE1_ke[(hp, c)])
                if c > 0:
                    tensor.wait_ge(S.h["vE"], vE1_k[(hp, c - 1)])
                elif hp > 0:
                    tensor.wait_ge(S.h["vE"], vE1_q[(hp - 1, 3)])
                i = tensor.matmul(P.PB.ap()[:, 0:512], P.swpb.ap()[:, :],
                                  kst[:, c % 2, :], start=True, stop=True)
                S.inc(i, "tE", 1)
            for c in range(4):
                tensor.wait_ge(S.h["gp"], gp_xho[hp])
                tensor.wait_ge(S.h["sE"],
                               sE1_qe[(hp, c - 1)] if c else sE1_ke[(hp, NKC - 1)])
                proj(0, hp, xh_own, c)
                tensor.wait_ge(S.h["sE"], sE1_qe[(hp, c)])
                tensor.wait_ge(S.h["vE"],
                               vE1_q[(hp, c - 1)] if c else vE1_k[(hp, NKC - 1)])
                i = tensor.matmul(P.PB.ap()[:, 0:512], P.swpb.ap()[:, :],
                                  qst[:, c % 2, :], start=True, stop=True)
                S.inc(i, "tE", 1)
            def vts(j):
                tensor.wait_ge(S.h["sE"], sE1_ve[(hp, j // 4)])
                if j > 0:
                    tensor.wait_ge(S.h["sE"], sE1_vte[(hp, j - 1)])
                elif hp > 0:
                    tensor.wait_ge(S.h["sE"], sE1_vte[(hp - 1, 31)])
                i = tensor.transpose(
                    P.PT.ap()[:, 0:128],
                    vst[:, (j // 4) % 2, (j % 4) * 128:(j % 4) * 128 + 128],
                    P.eyeb.ap()[:, :])
                S.inc(i, "tE", 1)

            for c in range(NKC):
                tensor.wait_ge(S.h["sE"],
                               sE1_ve[(hp, c - 1)] if c else sE1_qe[(hp, 3)])
                proj(2, hp, xh, c)
                if c >= 1:
                    for j in range(4 * (c - 1), 4 * c):
                        vts(j)
            for j in range(28, 32):
                vts(j)

    @blk.scalar
    def _(scalar):
        for hp in range(8):
            for c in range(NKC):
                scalar.wait_ge(S.h["tE"], tE1_kp[(hp, c)])
                if c >= 2:
                    scalar.wait_ge(S.h["vE"], vE1_k[(hp, c - 2)])
                elif hp > 0:
                    scalar.wait_ge(S.h["vE"], vE1_k[(hp - 1, 6 + c)])
                i = scalar.copy(kst[:, c % 2, :], P.PA.ap()[:, 0:512])
                S.inc(i, "sE", 1)
            for c in range(4):
                scalar.wait_ge(S.h["tE"], tE1_qp[(hp, c)])
                if c >= 2:
                    scalar.wait_ge(S.h["vE"], vE1_q[(hp, c - 2)])
                elif hp > 0:
                    scalar.wait_ge(S.h["vE"], vE1_q[(hp - 1, 2 + c)])
                i = scalar.copy(qst[:, c % 2, :], P.PA.ap()[:, 0:512])
                S.inc(i, "sE", 1)
            for c in range(NKC):
                scalar.wait_ge(S.h["tE"], tE1_vp[(hp, c)])
                if c >= 2:
                    scalar.wait_ge(S.h["tE"], tE1_vt[(hp, 4 * (c - 2) + 3)])
                i = scalar.copy(vst[:, c % 2, :], P.PA.ap()[:, 0:512])
                S.inc(i, "sE", 1)
                for j in range(4 * c, 4 * c + 4):
                    scalar.wait_ge(S.h["tE"], tE1_vt[(hp, j)])
                    if j == 0 and hp > 0:
                        scalar.wait_ge(S.h["gp"], gp_vo[(hp - 1, 1)])
                    if j == 16:
                        scalar.wait_ge(S.h["gp"], gp_vo[(hp, 0)])
                    i = scalar.copy(vtst[:, j % 16, :], P.PT.ap()[:, 0:128])
                    S.inc(i, "sE", 1)

    @blk.vector
    def _(vector):
        def rope(stg, c, tbl_c, out):
            vector.tensor_tensor(t1f[:, :], stg[:, c % 2, :],
                                 rkh[:, tbl_c % 2, 0, :], ALU.mult)
            vector.tensor_tensor(P.PB.ap()[:, 0:512], P.PB.ap()[:, 0:512],
                                 rkh[:, tbl_c % 2, 1, :], ALU.mult)
            vector.drain()
            i = vector.tensor_tensor(out, t1f[:, :], P.PB.ap()[:, 0:512],
                                     ALU.add)
            return i

        for hp in range(8):
            for c in range(NKC):
                vector.wait_ge(S.h["sE"], sE1_ke[(hp, c)])
                vector.wait_ge(S.h["tE"], tE1_ks[(hp, c)])
                vector.wait_ge(S.h["ldA"], ld_kt[(hp, c)])
                i = rope(kst, c, c, K_all[:, hp, ts(c, 512)])
                S.inc(i, "vE", 1)
            for c in range(4):
                vector.wait_ge(S.h["sE"], sE1_qe[(hp, c)])
                vector.wait_ge(S.h["tE"], tE1_qs[(hp, c)])
                vector.wait_ge(S.h["ldA"], ld_qt[(hp, c)])
                i = rope(qst, c, c, Q_all[:, hp, ts(c, 512)])
                S.inc(i, "vE", 1)

    regions4 = [(P.PA, 0), (P.PA, 512), (P.PB, 0), (P.PB, 512)]

    @blk.sync
    def _(sync):
        for i in range(NT):
            for g2 in range((L1_EXT[i] + 3) // 4):
                if (i, g2) == (0, 0):
                    sync.wait_ge(S.h["vE"], E1_vE_end)
                    sync.wait_ge(S.h["gp"], E1_gp_end)
                elif g2 >= 2:
                    sync.wait_ge(S.h["tE"], tE2_y[(i, grp_last_kt(i, g2 - 2))])
                elif i > 0:
                    pg = (L1_EXT[i - 1] + 3) // 4 - 2 + g2
                    if pg >= 0:
                        sync.wait_ge(S.h["tE"],
                                     tE2_y[(i - 1, grp_last_kt(i - 1, pg))])
                sync.wait_ge(S.h["ldA"], S.val("ldA"))
                src = P.vt_hbm.ap().rearrange("(tt p) f -> tt p f", p=128)
                n_t = min(4, L1_EXT[i] - g2 * 4)
                i_ = sync.dma_start(out=vtr[g2 % 2][:, 0:n_t, :],
                                    in_=src[g2 * 4:g2 * 4 + n_t].transpose([1, 0, 2]))
                S.inc(i_, "ldA", 16)

    @blk.tensor
    def _(tensor):
        for i in range(NT):
            ch = chunks_of(i)
            for q, (off, w) in enumerate(ch):
                reg = regions4[q % 4]
                if q >= 4:
                    tensor.wait_ge(S.h["sE"], sE2_e[(i, q - 4)])
                elif i > 0:
                    tensor.wait_ge(S.h["sE"], sE2_e[(i - 1, len(chunks_of(i - 1)) - 1)])
                else:
                    tensor.wait_ge(S.h["sE"], E1_sE_end)
                    tensor.wait_ge(S.h["vE"], E1_vE_end)
                i_ = None
                for fc in range(NKC):
                    i_ = tensor.matmul(reg[0].ap()[:, reg[1]:reg[1] + w],
                                       Q_all[:, fc, ts(i)],
                                       K_all[:, fc, off:off + w],
                                       start=(fc == 0), stop=(fc == NKC - 1))
                S.inc(i_, "tE", 1)
            for kt in range(L1_EXT[i]):
                tensor.wait_ge(S.h["sE"], sE2_e[(i, (kt * 128) // 512)])
                if kt > 0:
                    tensor.wait_ge(S.h["sE"], sE2_pe[(i, kt - 1)])
                elif i > 0:
                    tensor.wait_ge(S.h["sE"], sE2_pe[(i - 1, L1_EXT[i - 1] - 1)])
                i_ = tensor.transpose(P.PT.ap()[:, 0:128], pstr[:, ts(kt)],
                                      P.eyeb.ap()[:, :])
                S.inc(i_, "tE", 1)
                tensor.wait_ge(S.h["ldA"], ld2[(i, kt // 4)])
                tensor.wait_ge(S.h["sE"], sE2_pe[(i, kt)])
                if kt == 0 and i > 0:
                    tensor.wait_ge(S.h["vE"], vE2_y[i - 1])
                for half in range(2):
                    i_ = tensor.matmul(
                        P.PC.ap()[:, half * 512:half * 512 + 512],
                        ptb[:, kt % 2, :],
                        vtr[(kt // 4) % 2][:, kt % 4, half * 512:half * 512 + 512],
                        start=(kt == 0), stop=(kt == L1_EXT[i] - 1))
                S.inc(i_, "tE", 1)

    @blk.vector
    def _(vector):
        for i in range(NT):
            ch = chunks_of(i)
            nq = len(ch)
            for q, (off, w) in enumerate(ch):
                if (i, q) == (0, 0):
                    vector.drain()
                    vector.wait_ge(S.h["ldW"], P.E_ldW_end)
                    vector.wait_ge(S.h["gp"], E1_gp_end)
                reg = regions4[q % 4]
                nk = (w + 127) // 128
                for kk in range(nk):
                    mcol = midx[(i, (off // 128) + kk)]
                    vector.tensor_scalar(
                        msk[:, kk * 128:kk * 128 + 128], P.trilf.ap()[:, :],
                        mabs[:, mcol, 0:1], mabs[:, mcol, 1:2],
                        ALU.mult, ALU.add)
                vector.drain()
                vector.wait_ge(S.h["tE"], tE2_s[(i, q)])
                i_ = vector.scalar_tensor_tensor(
                    reg[0].ap()[:, reg[1]:reg[1] + w],
                    reg[0].ap()[:, reg[1]:reg[1] + w],
                    CBIG, msk[:, 0:w], ALU.add, ALU.mult)
                S.inc(i_, "vE", 1)
            vector.wait_ge(S.h["sE"], sE2_e[(i, nq - 1)])
            vector.reduce_sum(st[:, 52:53], lp2[:, 0:nq], axis=AX.X)
            vector.drain()
            i_ = vector.reciprocal(st[:, 53:54], st[:, 52:53])
            S.inc(i_, "vE", 1)
            vector.drain()
            vector.wait_ge(S.h["tE"], tE2_y[(i, L1_EXT[i] - 1)])
            xv = P.x1res[:, i, :].rearrange("p (d s) -> p s d", d=64, s=16)
            pv = P.PC.ap()[:, :].rearrange("p (s d) -> p s d", s=16, d=64)
            i_ = vector.scalar_tensor_tensor(
                xv, pv, st[:, 53:54], xv, ALU.mult, ALU.add)
            S.inc(i_, "vE", 1)

    @blk.scalar
    def _(scalar):
        for i in range(NT):
            for q, (off, w) in enumerate(chunks_of(i)):
                reg = regions4[q % 4]
                scalar.wait_ge(S.h["vE"], vE2_m[(i, q)])
                i_ = scalar.activation(
                    out=pstr[:, off:off + w],
                    in_=reg[0].ap()[:, reg[1]:reg[1] + w],
                    func=AF.Exp, scale=GAM, bias=st[:, 60:61],
                    accum_out=lp2[:, q:q + 1])
                S.inc(i_, "sE", 1)
            for kt in range(L1_EXT[i]):
                scalar.wait_ge(S.h["tE"], tE2_pt[(i, kt)])
                if kt >= 2:
                    scalar.wait_ge(S.h["tE"], tE2_y[(i, kt - 2)])
                elif i > 0:
                    scalar.wait_ge(S.h["tE"], tE2_y[(i - 1, L1_EXT[i - 1] - 1)])
                i_ = scalar.copy(ptb[:, kt % 2, :], P.PT.ap()[:, 0:128])
                S.inc(i_, "sE", 1)

    assert S.val("tE") == P.E_tE_end and S.val("vE") == P.E_vE_end \
        and S.val("sE") == P.E_sE_end and S.val("gp") == E1_gp_end
    P.taps["x1res_E"] = P.x1res[:, :, :]
    P.taps["K_all"] = K_all
    P.taps["Q_all"] = Q_all


def _phase_G(P):
    nc, S, blk = P.nc, P.S, P.blk

    @blk.scalar
    def _(scalar):
        for nm in ("tE", "vE", "sE", "cc", "gp", "ldA", "ldW"):
            if S.val(nm):
                S.wait(scalar, nm)
        out = P.OUT.rearrange("(tt p) d -> p tt d", p=128)
        i = scalar.dma_start(out=out, in_=P.x1res[:, :, :])
        S.inc(i, "st", 16)
        scalar.wait_ge(S.h["st"], S.val("st"))


def _emit_taps(P):
    nc, S, blk = P.nc, P.S, P.blk
    if not P.devout:
        return

    @blk.gpsimd
    def _(g):
        g.wait_ge(S.h["st"], S.val("st"))
        for nm, ap_out in P.devout.items():
            g.wait_ge(S.h["tp"], S.val("tp"))
            i = g.dma_start(out=ap_out, in_=P.taps[nm])
            S.inc(i, "tp", 16)
        g.wait_ge(S.h["tp"], S.val("tp"))


def kernel(**inputs):
    in_maps = [_prep_core_inputs(c, inputs) for c in range(8)]
    nc = build_program()
    res = run_bass_kernel_spmd(nc, in_maps, core_ids=list(range(8)))
    out = np.empty((B, T, D), dtype=np.float32)
    for c in range(8):
        b, m = c // 2, c % 2
        out[b, m * TOK:(m + 1) * TOK] = res.results[c]["out"]
    return out
